# version 20
# baseline (speedup 1.0000x reference)
import contextlib
import numpy as np

import concourse.bass as bass
import concourse.mybir as mybir
from concourse.bass_utils import run_bass_kernel_spmd

F32 = mybir.dt.float32
BF16 = mybir.dt.bfloat16
AF = mybir.ActivationFunctionType
ALU = mybir.AluOpType
AX = mybir.AxisListType.X

T = 2048
CD = 1024
NH = 16
HS = 64
ED = 512
NSH = 2
NRT = 16
NT = T // 128
EPS = 1e-5


class _Sem:
    __slots__ = ("h", "n")


def build_program():
    nc = bass.Bass("TRN2", target_bir_lowering=False)

    dx = nc.dram_tensor("x", [T, CD], F32, kind="ExternalInput").ap()
    dwq1 = nc.dram_tensor("wq1", [NH, CD, HS], F32, kind="ExternalInput").ap()
    dwk1 = nc.dram_tensor("wk1", [NH, CD, HS], F32, kind="ExternalInput").ap()
    dwq2 = nc.dram_tensor("wq2", [NH, CD, HS], F32, kind="ExternalInput").ap()
    dwk2 = nc.dram_tensor("wk2", [NH, CD, HS], F32, kind="ExternalInput").ap()
    dwv = nc.dram_tensor("wv", [NH, CD, HS], F32, kind="ExternalInput").ap()
    dgamma = nc.dram_tensor("gammab", [128, CD], F32, kind="ExternalInput").ap()
    dbeta = nc.dram_tensor("betab", [128, CD], F32, kind="ExternalInput").ap()
    dlamb = nc.dram_tensor("lambcol", [128, 1], F32, kind="ExternalInput").ap()
    dident = nc.dram_tensor("ident", [128, 128], F32, kind="ExternalInput").ap()
    dsup = nc.dram_tensor("supT", [NSH, CD, ED], F32, kind="ExternalInput").ap()
    dsdn = nc.dram_tensor("sdnT", [NSH, ED, CD], F32, kind="ExternalInput").ap()
    drup = nc.dram_tensor("rupT", [NRT, CD, ED], F32, kind="ExternalInput").ap()
    drdn = nc.dram_tensor("rdnT", [NRT, ED, CD], F32, kind="ExternalInput").ap()
    drw = nc.dram_tensor("rwT", [CD, NRT], F32, kind="ExternalInput").ap()
    dout = nc.dram_tensor("out", [T, CD], F32, kind="ExternalOutput").ap()

    ctx = contextlib.ExitStack()
    E = ctx.enter_context

    acc = E(nc.sbuf_tensor("acc", [128, NT, CD], F32))
    lnT = E(nc.sbuf_tensor("lnT", [128, 8, T], BF16))
    gam = E(nc.sbuf_tensor("gam", [128, CD], F32))
    bet = E(nc.sbuf_tensor("bet", [128, CD], F32))
    lamb = E(nc.sbuf_tensor("lamb", [128, 1], F32))
    nlam = E(nc.sbuf_tensor("nlam", [128, 1], F32))
    oml = E(nc.sbuf_tensor("oml", [128, 1], F32))
    idf = E(nc.sbuf_tensor("idf", [128, 128], F32))
    xt = E(nc.sbuf_tensor("xt", [128, 2, CD], F32))
    lnt = E(nc.sbuf_tensor("lnt", [128, 2, CD], F32))
    sums = E(nc.sbuf_tensor("sums", [128, 16], F32))
    invs = E(nc.sbuf_tensor("invs", [128, 16], F32))
    nli = E(nc.sbuf_tensor("nli", [128, 8], F32))
    tmp64 = E(nc.sbuf_tensor("tmp64", [128, 8, HS], F32))
    tmp64b = E(nc.sbuf_tensor("tmp64b", [128, 8, HS], F32))
    lnsc = E(nc.sbuf_tensor("lnsc", [128, 8], F32))
    stat = E(nc.sbuf_tensor("stat", [128, 6, 16], F32))
    gate = E(nc.sbuf_tensor("gate", [128, NT, NRT], F32))
    gl = E(nc.sbuf_tensor("gl", [128, 16], F32))
    gcur = E(nc.sbuf_tensor("gcur", [128, 2, 16], F32))
    gmsk = E(nc.sbuf_tensor("gmsk", [128, 16], F32))
    gsc = E(nc.sbuf_tensor("gsc", [128, 8], F32))
    rwS = E(nc.sbuf_tensor("rwS", [128, 8, NRT], F32))
    rwB = E(nc.sbuf_tensor("rwB", [128, 8, NRT], BF16))

    AR_COLS = 18500
    arena = E(nc.sbuf_tensor("arena", [128, AR_COLS], F32))
    _off = [0]

    def take(cols):
        o = _off[0]
        _off[0] += cols
        assert _off[0] <= AR_COLS, f"arena overflow {_off[0]}"
        return arena[:, o:o + cols]

    def take_bf16(bf16_elems):
        assert bf16_elems % 2 == 0
        return take(bf16_elems // 2).bitcast(BF16)

    qk = [take_bf16(T) for _ in range(4)]
    vext = take_bf16(NT * 130).rearrange("p (n c) -> p n c", n=NT)
    wat = [[take_bf16(8 * 128).rearrange("p (c m) -> p c m", c=8) for _ in range(5)]
           for _ in range(2)]
    wstg = [take(8 * HS).rearrange("p (c d) -> p c d", c=8) for _ in range(2)]
    e12 = [take_bf16(2048) for _ in range(3)]
    e1c = [b[:, 0:1024] for b in e12]
    e2c = [b[:, 1024:2048] for b in e12]
    NEB = 3
    oT1 = take(1024)
    oT2 = take(1024)
    attn_cols = _off[0]
    assert attn_cols <= AR_COLS

    _off[0] = 0
    wup = [take_bf16(8 * ED).rearrange("p (c e) -> p c e", c=8) for _ in range(2)]
    wdn = [take_bf16(4 * CD).rearrange("p (c h) -> p c h", c=4) for _ in range(2)]
    fstg = [take(2 * ED).rearrange("p (c e) -> p c e", c=2) for _ in range(2)]
    actb = [take_bf16(4 * T).rearrange("p (c t) -> p c t", c=4) for _ in range(2)]
    ffn_cols = _off[0]
    assert ffn_cols <= AR_COLS, f"ffn arena {ffn_cols}"
    junk = actb[1].bitcast(F32)[:, 1, :]
    ctmp = actb[0].bitcast(F32)[:, 0, :]

    pAB = E(nc.psum_tensor("pAB", [128, 2048], F32))
    pA = pAB[:, 0:1024]
    pB = pAB[:, 1024:2048]
    pC = E(nc.psum_tensor("pC", [128, 1024], F32))
    pD = E(nc.psum_tensor("pD", [128, 1024], F32))
    pA3 = pA.rearrange("p (n c) -> p n c", n=8)
    pB3 = pB.rearrange("p (n c) -> p n c", n=8)
    pC3 = pC.rearrange("p (n c) -> p n c", n=8)
    pD3 = pD.rearrange("p (n c) -> p n c", n=8)

    prog = {k: [] for k in ("sync", "vector", "scalar", "tensor", "gpsimd")}

    def em(eng, fn):
        prog[eng].append(fn)

    sems = {}

    def mk(name):
        s = _Sem()
        s.h = E(nc.semaphore(name))
        s.n = 0
        sems[name] = s
        return s

    s_const = mk("s_const")
    s_xld = [mk("s_xldA"), mk("s_xldB")]
    s_xfree = mk("s_xfree")
    s_pre = mk("s_pre")
    s_v = mk("s_v")
    s_s = mk("s_s")
    s_lnr = mk("s_lnr")
    s_lnfree = mk("s_lnfree")
    s_lntr = mk("s_lntr")
    s_lnTd = mk("s_lnTd")
    s_wld = [mk("s_wldA"), mk("s_wldB")]
    s_wstgf = mk("s_wstgf")
    s_wcast = mk("s_wcast")
    s_wfree = mk("s_wfree")
    s_proj = mk("s_proj")
    s_projd = mk("s_projd")
    s_st = mk("s_st")
    s_exp = mk("s_exp")
    s_av = mk("s_av")
    s_oev = mk("s_oev")
    s_otr = mk("s_otr")
    s_epi = mk("s_epi")
    s_ep1 = mk("s_ep1")
    s_ep2 = mk("s_ep2")
    s_rl = mk("s_rl")
    s_gate = mk("s_gate")
    s_fld = [mk("s_fldA"), mk("s_fldB")]
    s_fstgf = mk("s_fstgf")
    s_fcast = mk("s_fcast")
    s_ffree = mk("s_ffree")
    s_up = mk("s_up")
    s_act = mk("s_act")
    s_dn = mk("s_dn")
    s_gacc = mk("s_gacc")
    s_out = mk("s_out")

    def W(eng, s, val=None):
        v = s.n if val is None else val
        v = max(v, 0)
        em(eng, lambda e, _h=s.h, _v=v: e.wait_ge(_h, _v))

    def I(eng, fn, s=None, k=1):
        if s is None:
            em(eng, fn)
        else:
            s.n += k
            em(eng, lambda e, _f=fn, _h=s.h, _k=k: _f(e).then_inc(_h, _k))

    def DMA(out_ap, in_ap, s):
        I("sync", lambda e, _o=out_ap, _i=in_ap: e.dma_start(out=_o, in_=_i), s, 16)

    DMA(lamb[:], dlamb, s_const)
    DMA(idf[:], dident, s_const)
    DMA(gam[:], dgamma, s_const)
    DMA(bet[:], dbeta, s_const)
    DMA(rwS[:], drw.rearrange("(c p) r -> p c r", p=128), s_const)
    W("scalar", s_const)
    I("scalar", lambda e: e.activation(oml[:], lamb[:], AF.Copy, scale=-1.0, bias=1.0), s_s)
    I("scalar", lambda e: e.activation(nlam[:], lamb[:], AF.Copy, scale=-1.0), s_s)
    W("vector", s_s)
    W("vector", s_const)
    I("vector", lambda e: e.tensor_scalar(gam[:], gam[:], oml[:], None, ALU.mult), s_v)
    I("vector", lambda e: e.tensor_scalar(bet[:], bet[:], oml[:], None, ALU.mult), s_v)
    I("vector", lambda e: e.tensor_copy(rwB[:], rwS[:]), s_pre)
    I("vector", lambda e: e.memset(vext[:], 1.0), s_pre)

    s_sums = stat[:, 0, :]
    s_nmean = stat[:, 1, :]
    s_sqs = stat[:, 2, :]
    s_var = stat[:, 3, :]
    s_rstd = stat[:, 4, :]
    s_nb = stat[:, 5, :]

    def emit_ln_stats_tile(src_ap, tt):
        I("vector", lambda e, _t=tt, _s=src_ap: e.reduce_sum(s_sums[:, _t:_t + 1], _s, axis=AX),
          s_v if tt == NT - 1 else None)
        I("scalar", lambda e, _t=tt, _s=src_ap: e.activation(junk[:], _s, AF.Square,
                                                             accum_out=s_sqs[:, _t:_t + 1]),
          s_s)

    def emit_ln_stats_finish():
        W("vector", s_v)
        W("vector", s_s)
        I("vector", lambda e: e.tensor_scalar(s_nmean[:], s_sums[:], -1.0 / CD, None, ALU.mult), s_v)
        W("vector", s_v)
        I("vector", lambda e: e.tensor_tensor(s_var[:], s_nmean[:], s_nmean[:], ALU.mult), s_v)
        I("vector", lambda e: e.tensor_scalar(s_sqs[:], s_sqs[:], 1.0 / CD, EPS, ALU.mult, ALU.add), s_v)
        W("vector", s_v)
        I("vector", lambda e: e.tensor_sub(s_var[:], s_sqs[:], s_var[:]), s_v)
        W("scalar", s_v)
        I("scalar", lambda e: e.activation(s_var[:], s_var[:], AF.Ln), s_s)
        W("scalar", s_s)
        I("scalar", lambda e: e.activation(s_rstd[:], s_var[:], AF.Exp, scale=-0.5), s_s)
        W("vector", s_s)
        I("vector", lambda e: e.tensor_tensor(s_nb[:], s_nmean[:], s_rstd[:], ALU.mult), s_v)
        W("scalar", s_v)

    def emit_ln_scale_tile(src_ap, tt, lnbuf, transposed=True):
        if transposed and tt >= 2:
            W("scalar", s_lntr, s_lntr.n - 1)
        I("scalar", lambda e, _t=tt, _s=src_ap, _b=lnbuf: e.activation(
            lnt[:, _b, :], _s, AF.Identity, scale=s_rstd[:, _t:_t + 1],
            bias=s_nb[:, _t:_t + 1]), s_lnr)

    def emit_ln_transpose(tt, lnbuf):
        pp = pC3 if tt % 2 == 0 else pD3
        W("tensor", s_lnr)
        if tt >= 2:
            W("tensor", s_lnTd, s_lnTd.n - 16)
        for c in range(8):
            last = (c == 7)
            I("tensor",
              lambda e, _c=c, _pp=pp, _b=lnbuf: e.transpose(
                  _pp[:, _c, :], lnt[:, _b, _c * 128:(_c + 1) * 128], idf[:]),
              s_lntr if last else None)
        W("vector", s_lntr)
        I("vector",
          lambda e, _pp=pp, _t=tt: e.tensor_copy(
              lnT[:, :, _t * 128:(_t + 1) * 128], _pp[:, :, :]),
          s_lnTd, 16)

    W("vector", s_pre)
    for tt in range(NT):
        buf = tt % 2
        if tt >= 2:
            W("sync", s_xfree, s_xfree.n - 1)
        DMA(xt[:, buf, :], dx.rearrange("(n p) c -> p n c", p=128)[:, tt, :], s_xld[buf])
        W("vector", s_xld[buf])
        W("scalar", s_xld[buf])
        emit_ln_stats_tile(xt[:, buf, :], tt)
        W("vector", s_s)
        I("vector", lambda e: e.memset(lnsc[:, 7:8], 0.0), s_xfree)
    emit_ln_stats_finish()
    for tt in range(NT):
        buf = tt % 2
        W("sync", s_xfree, s_xfree.n - 1)
        DMA(xt[:, buf, :], dx.rearrange("(n p) c -> p n c", p=128)[:, tt, :], s_xld[buf])
        W("scalar", s_xld[buf])
        emit_ln_scale_tile(xt[:, buf, :], tt, buf)
        W("vector", s_lnr)
        I("vector", lambda e: e.memset(lnsc[:, 7:8], 0.0), s_xfree)
        emit_ln_transpose(tt, buf)

    WSRC = [dwq1, dwq2, dwk1, dwk2, dwv]

    def emit_attn_wload(p):
        wb = wat[p % 2]
        h0, h1 = 2 * p, 2 * p + 1
        units = [(0, 0, dwq1, h0), (0, 1, dwq2, h0),
                 (1, 0, dwq1, h1), (1, 1, dwq2, h1),
                 (2, 0, dwk1, h0), (2, 1, dwk2, h0),
                 (3, 0, dwk1, h1), (3, 1, dwk2, h1),
                 (4, 0, dwv, h0), (4, 1, dwv, h1)]
        if p >= 1:
            W("sync", s_wfree, p - 1)
        for u, (di, half, src, hh) in enumerate(units):
            sb = u % 2
            if p > 0 or u >= 2:
                W("sync", s_wstgf, s_wstgf.n - 1)
            DMA(wstg[sb][:], src[hh].rearrange("(c p) d -> p c d", p=128), s_wld[sb])
            W("vector", s_wld[sb])
            if p >= 1:
                W("vector", s_wfree, p - 1)
            I("vector",
              lambda e, _d=wb[di], _h=half, _s=wstg[sb]: e.tensor_copy(
                  _d[:, :, _h * HS:(_h + 1) * HS], _s[:]),
              s_wstgf)
        I("vector", lambda e: e.memset(lnsc[:, 7:8], 0.0), s_wcast)

    def emit_proj(p):
        wb = wat[p % 2]
        W("tensor", s_wcast, p + 1)
        W("tensor", s_exp)
        lnA_base = s_lnTd.n - 16 * NT
        W("vector", s_av)
        nchain = 0
        for ts4 in range(4):
            W("tensor", s_lnTd, lnA_base + 64 * (ts4 + 1))
            for di in range(4):
                half = nchain % 2
                ppo = pA[:, half * 512:(half + 1) * 512]
                if nchain >= 2:
                    W("tensor", s_projd, s_projd.n - 1)
                for hc in range(8):
                    I("tensor",
                      lambda e, _w=wb[di], _hc=hc, _ts=ts4, _o=ppo: e.matmul(
                          _o, _w[:, _hc, :], lnT[:, _hc, _ts * 512:(_ts + 1) * 512],
                          start=(_hc == 0), stop=(_hc == 7)),
                      s_proj if hc == 7 else None)
                W("vector", s_proj)
                I("vector",
                  lambda e, _d=qk[di], _ts=ts4, _o=ppo: e.tensor_copy(
                      _d[:, _ts * 512:(_ts + 1) * 512], _o),
                  s_projd)
                nchain += 1
        I("vector", lambda e: e.memset(vext[:, :, 64:65], 1.0), s_v)
        I("vector", lambda e: e.memset(vext[:, :, 129:130], 1.0), s_v)
        for tt in range(NT):
            half = tt % 2
            ppo = pB[:, half * 512:half * 512 + 128]
            W("tensor", s_lnTd, lnA_base + 16 * (tt + 1))
            if tt >= 2:
                W("tensor", s_projd, s_projd.n - 1)
            for hc in range(8):
                I("tensor",
                  lambda e, _hc=hc, _tt=tt, _o=ppo: e.matmul(
                      _o, lnT[:, _hc, _tt * 128:(_tt + 1) * 128], wb[4][:, _hc, :],
                      start=(_hc == 0), stop=(_hc == 7)),
                  s_proj if hc == 7 else None)
            W("vector", s_proj)
            I("vector",
              lambda e, _tt=tt, _o=ppo: e.tensor_copy(vext[:, _tt, 0:64], _o[:, 0:64]),
              None)
            I("vector",
              lambda e, _tt=tt, _o=ppo: e.tensor_copy(vext[:, _tt, 65:129], _o[:, 64:128]),
              s_projd)
        I("tensor", lambda e: e.nop(), s_wfree)

    def emit_attn_inner(p, hl, Hh, grp):
        qT = qk[hl]
        kT = qk[2 + hl]
        h = 2 * p + hl
        tbase = Hh * 1024
        vlo, vhi = (0, 65) if hl == 0 else (65, 130)
        for sc in range(17):
            scp = sc - 1
            buf = scp % NEB
            if sc < 16:
                if sc == 0:
                    W("tensor", s_projd)
                W("tensor", s_exp, s_exp.n - 1)
                for half in range(2):
                    I("tensor",
                      lambda e, _sc=sc, _hf=half: e.matmul(
                          pA[:, _hf * 512:(_hf + 1) * 512],
                          kT[0:64, _sc * 128:(_sc + 1) * 128],
                          qT[0:64, tbase + _hf * 512: tbase + (_hf + 1) * 512],
                          start=True, stop=True),
                      s_st if half == 1 else None)
                W("tensor", s_exp)
                for half in range(2):
                    I("tensor",
                      lambda e, _sc=sc, _hf=half: e.matmul(
                          pB[:, _hf * 512:(_hf + 1) * 512],
                          kT[64:128, _sc * 128:(_sc + 1) * 128],
                          qT[64:128, tbase + _hf * 512: tbase + (_hf + 1) * 512],
                          start=True, stop=True),
                      s_st if half == 1 else None)
            if sc >= 1:
                if sc == 16:
                    W("tensor", s_exp)
                if scp == 0:
                    if grp == 0:
                        W("tensor", s_lnTd)
                    W("tensor", s_oev)
                    W("tensor", s_ep1)
                    W("tensor", s_ep2)
                for half in range(2):
                    I("tensor",
                      lambda e, _sc=scp, _hf=half, _b=buf: e.matmul(
                          pC[0:65, _hf * 512:(_hf + 1) * 512],
                          vext[:, _sc, vlo:vhi],
                          e1c[_b][:, _hf * 512:(_hf + 1) * 512],
                          start=(_sc == 0), stop=(_sc == 15)),
                      None)
                for half in range(2):
                    I("tensor",
                      lambda e, _sc=scp, _hf=half, _b=buf: e.matmul(
                          pD[0:65, _hf * 512:(_hf + 1) * 512],
                          vext[:, _sc, vlo:vhi],
                          e2c[_b][:, _hf * 512:(_hf + 1) * 512],
                          start=(_sc == 0), stop=(_sc == 15)),
                      s_av if half == 1 else None)
            if sc < 16:
                W("scalar", s_st, s_st.n - 1)
                if sc == 0:
                    W("scalar", s_av, s_av.n)
                elif sc == 1:
                    W("scalar", s_av, s_av.n - 1)
                else:
                    W("scalar", s_av, s_av.n - 2)
                I("scalar",
                  lambda e, _b=sc % NEB: e.activation(e1c[_b][:], pA[:, :], AF.Exp,
                                                      scale=0.125),
                  s_exp)
                W("scalar", s_st)
                I("scalar",
                  lambda e, _b=sc % NEB: e.activation(e2c[_b][:], pB[:, :], AF.Exp,
                                                      scale=0.125),
                  s_exp)
        W("vector", s_av)
        I("vector", lambda e: e.tensor_copy(oT1[0:65, :], pC[0:65, :]), None)
        I("vector", lambda e: e.tensor_copy(oT2[0:65, :], pD[0:65, :]), s_oev)
        W("tensor", s_oev)
        for j in range(8):
            I("tensor",
              lambda e, _j=j: e.transpose(pC3[:, _j, 0:65],
                                          oT1[0:65, _j * 128:(_j + 1) * 128],
                                          idf[0:65, 0:65]),
              None)
        for j in range(8):
            I("tensor",
              lambda e, _j=j: e.transpose(pD3[:, _j, 0:65],
                                          oT2[0:65, _j * 128:(_j + 1) * 128],
                                          idf[0:65, 0:65]),
              s_otr if j == 7 else None)
        W("vector", s_otr)
        I("vector", lambda e: e.tensor_copy(sums[:, 0:8], pC3[:, :, 64:65].rearrange("p n c -> p (n c)")), None)
        I("vector", lambda e: e.tensor_copy(sums[:, 8:16], pD3[:, :, 64:65].rearrange("p n c -> p (n c)")), s_v)
        W("vector", s_v)
        I("vector", lambda e: e.reciprocal(invs[:], sums[:]), s_v)
        W("vector", s_v)
        I("vector", lambda e: e.tensor_scalar(nli[:], invs[:, 8:16], nlam[:], None, ALU.mult), s_v)
        W("vector", s_v)
        I("vector", lambda e: e.tensor_tensor(
            tmp64[:], pC3[:, :, 0:64],
            invs[:, 0:8].rearrange("p n -> p n ()").broadcast_to([128, 8, HS]),
            ALU.mult), s_ep1)
        I("vector", lambda e: e.tensor_tensor(
            tmp64b[:], pD3[:, :, 0:64],
            nli[:].rearrange("p n -> p n ()").broadcast_to([128, 8, HS]),
            ALU.mult), s_ep2)
        W("vector", s_ep1)
        W("vector", s_ep2)
        I("vector", lambda e: e.tensor_tensor(
            acc[:, Hh * 8:(Hh + 1) * 8, h * HS:(h + 1) * HS],
            tmp64[:], tmp64b[:], ALU.add), s_epi)

    for p in range(8):
        emit_attn_wload(p)
        emit_proj(p)
        for hl in range(2):
            for Hh in range(2):
                emit_attn_inner(p, hl, Hh, p * 4 + hl * 2 + Hh)

    W("sync", s_epi)
    W("vector", s_epi)
    W("scalar", s_epi)
    for tt in range(NT):
        emit_ln_stats_tile(acc[:, tt, :], tt)
    emit_ln_stats_finish()
    for tt in range(NT):
        buf = tt % 2
        W("sync", s_xfree, s_xfree.n if tt < 2 else s_xfree.n - 1)
        DMA(xt[:, buf, :], dx.rearrange("(n p) c -> p n c", p=128)[:, tt, :], s_xld[buf])
        a_t = acc[:, tt, :]
        if tt >= 2:
            W("scalar", s_v, s_v.n - 3)
        emit_ln_scale_tile(a_t, tt, buf, transposed=False)
        W("vector", s_lnr)
        W("vector", s_xld[buf])
        I("vector", lambda e, _b=buf: e.tensor_tensor(lnt[:, _b, :], lnt[:, _b, :], gam[:], ALU.mult), s_v)
        I("vector", lambda e, _b=buf: e.tensor_tensor(ctmp[:], xt[:, _b, :], bet[:], ALU.add), s_v)
        W("vector", s_v)
        I("vector", lambda e, _a=a_t, _b=buf: e.tensor_tensor(_a, lnt[:, _b, :], ctmp[:], ALU.add), s_v)
        I("vector", lambda e: e.memset(lnsc[:, 7:8], 0.0), s_xfree)
        W("scalar", s_v)
        emit_ln_stats_tile(acc[:, tt, :], tt)
    emit_ln_stats_finish()
    for tt in range(NT):
        buf = tt % 2
        emit_ln_scale_tile(acc[:, tt, :], tt, buf)
        emit_ln_transpose(tt, buf)

    W("tensor", s_lnTd)
    W("tensor", s_lnr)
    for tt in range(NT):
        slot = pB3[:, tt // 2, (tt % 2) * 32:(tt % 2) * 32 + NRT]
        for hc in range(8):
            I("tensor",
              lambda e, _hc=hc, _tt=tt, _sl=slot: e.matmul(
                  _sl, lnT[:, _hc, _tt * 128:(_tt + 1) * 128], rwB[:, _hc, :],
                  start=(_hc == 0), stop=(_hc == 7)),
              s_rl if hc == 7 else None)
    glL = gl3 = None
    pslots = pB.rearrange("p (q b x) -> p q b x", q=8, b=4)[:, :, 0:2, 0:NRT]
    gall = gate
    gm1 = E(nc.sbuf_tensor("gm1", [128, NT, 1], F32))
    gcur2 = E(nc.sbuf_tensor("gcur2", [128, NT, NRT], F32))
    gmskb = E(nc.sbuf_tensor("gmskb", [128, NT, NRT], F32))
    gsums = E(nc.sbuf_tensor("gsums", [128, NT, 1], F32))
    ginv = E(nc.sbuf_tensor("ginv", [128, NT, 1], F32))
    W("vector", s_rl)
    ga3 = gall[:]
    I("vector", lambda e: e.tensor_copy(ga3, pslots), s_v)
    W("vector", s_v)
    I("vector", lambda e: e.reduce_max(gm1[:], ga3, axis=AX), s_v)
    W("vector", s_v)
    I("vector", lambda e: e.tensor_sub(gcur2[:], ga3, gm1[:].broadcast_to([128, NT, NRT])), s_v)
    W("scalar", s_v)
    I("scalar", lambda e: e.activation(ga3, gcur2[:], AF.Exp), s_s)
    W("vector", s_s)
    I("vector", lambda e: e.reduce_sum(gsums[:], ga3, axis=AX), s_v)
    I("vector", lambda e: e.tensor_copy(gcur2[:], ga3), s_v)
    W("vector", s_v)
    for it in range(3):
        I("vector", lambda e: e.reduce_max(gm1[:], gcur2[:], axis=AX), s_v)
        W("vector", s_v)
        I("vector", lambda e: e.tensor_tensor(gmskb[:], gcur2[:],
                                              gm1[:].broadcast_to([128, NT, NRT]), ALU.is_ge), s_v)
        W("vector", s_v)
        I("vector", lambda e: e.scalar_tensor_tensor(gcur2[:], gmskb[:], -2.0, gcur2[:],
                                                     ALU.mult, ALU.add), s_v)
        W("vector", s_v)
    I("vector", lambda e: e.reduce_max(gm1[:], gcur2[:], axis=AX), s_v)
    I("vector", lambda e: e.reciprocal(ginv[:], gsums[:]), s_v)
    W("vector", s_v)
    I("vector", lambda e: e.tensor_tensor(gmskb[:], ga3,
                                          gm1[:].broadcast_to([128, NT, NRT]), ALU.is_ge), s_v)
    W("vector", s_v)
    I("vector", lambda e: e.tensor_mul(gmskb[:], gmskb[:], ga3), s_v)
    W("vector", s_v)
    I("vector", lambda e: e.tensor_tensor(ga3, gmskb[:],
                                          ginv[:].broadcast_to([128, NT, NRT]), ALU.mult), s_gate)

    def emit_ffn_wload(r, src_up, src_dn, idx):
        wb_u, wb_d = wup[r % 2], wdn[r % 2]
        if r >= 2:
            W("sync", s_ffree, r - 1)
        for q in range(4):
            sb = q % 2
            if r > 0 or q >= 2:
                W("sync", s_fstgf, s_fstgf.n - 1)
            DMA(fstg[sb][:], src_up[idx][q * 256:(q + 1) * 256].rearrange(
                "(c p) e -> p c e", p=128), s_fld[sb])
            W("vector", s_fld[sb])
            if r >= 2:
                W("vector", s_ffree, r - 1)
            I("vector",
              lambda e, _q=q, _sb=sb, _w=wb_u: e.tensor_copy(
                  _w[:, _q * 2:(_q + 1) * 2, :], fstg[_sb][:]),
              s_fstgf)
        for q in range(4):
            sb = q % 2
            W("sync", s_fstgf, s_fstgf.n - 1)
            DMA(fstg[sb][:].rearrange("p c e -> p (c e)"),
                src_dn[idx][q * 128:(q + 1) * 128].rearrange("(c p) h -> p (c h)", p=128),
                s_fld[sb])
            W("vector", s_fld[sb])
            I("vector",
              lambda e, _q=q, _sb=sb, _w=wb_d: e.tensor_copy(
                  _w[:, _q, :], fstg[_sb][:].rearrange("p c e -> p (c e)")),
              s_fstgf)
        I("vector", lambda e: e.memset(lnsc[:, 7:8], 0.0), s_fcast)

    def emit_expert(r, gated, gcol_idx):
        wb_u, wb_d = wup[r % 2], wdn[r % 2]
        ab = actb[r % 2]
        W("tensor", s_fcast, r + 1)
        if r == 0:
            W("tensor", s_lnTd)
            W("tensor", s_epi)
        nup = 0
        for tc in range(4):
            for es in range(4):
                half = nup % 2
                ppo = pA[:, half * 512:(half + 1) * 512]
                if r > 0 or nup >= 2:
                    W("tensor", s_act, s_act.n - 1)
                if r >= 2 and nup == 0:
                    W("tensor", s_dn)
                for hc in range(8):
                    I("tensor",
                      lambda e, _hc=hc, _es=es, _tc=tc, _o=ppo: e.matmul(
                          _o, wb_u[:, _hc, _es * 128:(_es + 1) * 128],
                          lnT[:, _hc, _tc * 512:(_tc + 1) * 512],
                          start=(_hc == 0), stop=(_hc == 7)),
                      s_up if hc == 7 else None)
                W("scalar", s_up)
                if r >= 2 and nup == 0:
                    W("scalar", s_dn)
                I("scalar",
                  lambda e, _es=es, _tc=tc, _o=ppo, _a=ab: e.activation(
                      _a[:, _es, _tc * 512:(_tc + 1) * 512], _o, AF.Gelu),
                  s_act)
                nup += 1
        act_base = s_act.n - 16
        for tt in range(NT):
            pp = pC if tt % 2 == 0 else pD
            W("tensor", s_act, act_base + 4 * (tt // 4 + 1))
            if tt < 2:
                if r > 0:
                    W("tensor", s_gacc)
                else:
                    W("tensor", s_oev)
            else:
                W("tensor", s_gacc, s_gacc.n - 1)
            for half in range(2):
                for ec in range(4):
                    I("tensor",
                      lambda e, _ec=ec, _tt=tt, _hf=half, _pp=pp: e.matmul(
                          _pp[:, _hf * 512:(_hf + 1) * 512],
                          ab[:, _ec, _tt * 128:(_tt + 1) * 128],
                          wb_d[:, _ec, _hf * 512:(_hf + 1) * 512],
                          start=(_ec == 0), stop=(_ec == 3)),
                      s_dn if (half == 1 and ec == 3) else None)
            W("vector", s_dn)
            if gated:
                I("vector",
                  lambda e, _tt=tt, _pp=pp, _g=gcol_idx: e.scalar_tensor_tensor(
                      acc[:, _tt, :], _pp[:], gate[:, _tt, _g:_g + 1], acc[:, _tt, :],
                      ALU.mult, ALU.add),
                  s_gacc)
            else:
                I("vector",
                  lambda e, _tt=tt, _pp=pp: e.tensor_tensor(
                      acc[:, _tt, :], acc[:, _tt, :], _pp[:], ALU.add),
                  s_gacc)
        I("tensor", lambda e: e.nop(), s_ffree)

    W("vector", s_av)
    W("vector", s_otr)
    W("sync", s_epi)
    W("sync", s_otr)
    order = [("s", 0), ("s", 1)] + [("r", i) for i in range(NRT)]

    def wl(r):
        kind, idx = order[r]
        if kind == "s":
            emit_ffn_wload(r, dsup, dsdn, idx)
        else:
            emit_ffn_wload(r, drup, drdn, idx)

    wl(0)
    wl(1)
    for r, (kind, idx) in enumerate(order):
        if r == 2:
            W("vector", s_gate)
        emit_expert(r, gated=(kind == "r"), gcol_idx=idx)
        if r + 2 < len(order):
            wl(r + 2)

    gacc_base = s_gacc.n - NT
    for tt in range(NT):
        W("sync", s_gacc, gacc_base + tt + 1)
        DMA(dout.rearrange("(n p) c -> p n c", p=128)[:, tt, :], acc[:, tt, :], s_out)
    W("sync", s_out)

    with nc.Block() as block:
        @block.sync
        def _(e):
            for f in prog["sync"]:
                f(e)

        @block.vector
        def _(e):
            for f in prog["vector"]:
                f(e)

        @block.scalar
        def _(e):
            for f in prog["scalar"]:
                f(e)

        @block.tensor
        def _(e):
            for f in prog["tensor"]:
                f(e)

        @block.gpsimd
        def _(e):
            for f in prog["gpsimd"]:
                f(e)

    ctx.close()
    return nc


_CACHE = {}


def kernel(L0_attn_x, L0_attn_wq1, L0_attn_wk1, L0_attn_wq2, L0_attn_wk2, L0_attn_wv,
           L0_attn_ln_gamma, L0_attn_ln_beta, L0_attn_lamb,
           L0_ffn_shared_up, L0_ffn_shared_down, L0_ffn_routed_up, L0_ffn_routed_down,
           L0_ffn_router_w):
    x = np.ascontiguousarray(np.asarray(L0_attn_x, dtype=np.float32))
    B = x.shape[0]
    common = {
        "wq1": np.ascontiguousarray(L0_attn_wq1, np.float32),
        "wk1": np.ascontiguousarray(L0_attn_wk1, np.float32),
        "wq2": np.ascontiguousarray(L0_attn_wq2, np.float32),
        "wk2": np.ascontiguousarray(L0_attn_wk2, np.float32),
        "wv": np.ascontiguousarray(L0_attn_wv, np.float32),
        "gammab": np.ascontiguousarray(
            np.broadcast_to(np.asarray(L0_attn_ln_gamma, np.float32)[None, :], (128, CD))),
        "betab": np.ascontiguousarray(
            np.broadcast_to(np.asarray(L0_attn_ln_beta, np.float32)[None, :], (128, CD))),
        "lambcol": np.ascontiguousarray(
            np.broadcast_to(np.asarray(L0_attn_lamb, np.float32).reshape(1, 1), (128, 1))),
        "ident": np.eye(128, dtype=np.float32),
        "supT": np.ascontiguousarray(np.asarray(L0_ffn_shared_up, np.float32).transpose(0, 2, 1)),
        "sdnT": np.ascontiguousarray(np.asarray(L0_ffn_shared_down, np.float32).transpose(0, 2, 1)),
        "rupT": np.ascontiguousarray(np.asarray(L0_ffn_routed_up, np.float32).transpose(0, 2, 1)),
        "rdnT": np.ascontiguousarray(np.asarray(L0_ffn_routed_down, np.float32).transpose(0, 2, 1)),
        "rwT": np.ascontiguousarray(np.asarray(L0_ffn_router_w, np.float32).T),
    }
    if "nc" not in _CACHE:
        _CACHE["nc"] = build_program()
    nc = _CACHE["nc"]
    in_maps = [dict(common, x=np.ascontiguousarray(x[b])) for b in range(B)]
    res = run_bass_kernel_spmd(nc, in_maps, core_ids=list(range(B)))
    return np.stack([res.results[b]["out"] for b in range(B)], axis=0)


if __name__ == "__main__":
    np.random.seed(0)
    import reference as ref
    inp = ref.setup_inputs()
    inp = {k: np.asarray(v) for k, v in inp.items()}
    got = kernel(**inp)
    exp = np.asarray(ref.reference(**{k: v for k, v in inp.items()}))
    err = np.sqrt(np.mean((got - exp) ** 2) / np.mean(exp ** 2))
    print("rel l2:", err, "max abs diff:", np.abs(got - exp).max())


# version 22 (final)
# speedup vs baseline: 1.0296x; 1.0296x over previous
import contextlib
import numpy as np

import concourse.bass as bass
import concourse.mybir as mybir
from concourse.bass_utils import run_bass_kernel_spmd

F32 = mybir.dt.float32
BF16 = mybir.dt.bfloat16
AF = mybir.ActivationFunctionType
ALU = mybir.AluOpType
AX = mybir.AxisListType.X

T = 2048
CD = 1024
NH = 16
HS = 64
ED = 512
NSH = 2
NRT = 16
NT = T // 128
EPS = 1e-5


class _Sem:
    __slots__ = ("h", "n")


def build_program():
    nc = bass.Bass("TRN2", target_bir_lowering=False)

    dx = nc.dram_tensor("x", [T, CD], F32, kind="ExternalInput").ap()
    dwq1 = nc.dram_tensor("wq1", [NH, CD, HS], F32, kind="ExternalInput").ap()
    dwk1 = nc.dram_tensor("wk1", [NH, CD, HS], F32, kind="ExternalInput").ap()
    dwq2 = nc.dram_tensor("wq2", [NH, CD, HS], F32, kind="ExternalInput").ap()
    dwk2 = nc.dram_tensor("wk2", [NH, CD, HS], F32, kind="ExternalInput").ap()
    dwv = nc.dram_tensor("wv", [NH, CD, HS], F32, kind="ExternalInput").ap()
    dgamma = nc.dram_tensor("gammab", [128, CD], F32, kind="ExternalInput").ap()
    dbeta = nc.dram_tensor("betab", [128, CD], F32, kind="ExternalInput").ap()
    dlamb = nc.dram_tensor("lambcol", [128, 1], F32, kind="ExternalInput").ap()
    dident = nc.dram_tensor("ident", [128, 128], F32, kind="ExternalInput").ap()
    dsup = nc.dram_tensor("supT", [NSH, CD, ED], F32, kind="ExternalInput").ap()
    dsdn = nc.dram_tensor("sdnT", [NSH, ED, CD], F32, kind="ExternalInput").ap()
    drup = nc.dram_tensor("rupT", [NRT, CD, ED], F32, kind="ExternalInput").ap()
    drdn = nc.dram_tensor("rdnT", [NRT, ED, CD], F32, kind="ExternalInput").ap()
    drw = nc.dram_tensor("rwT", [CD, NRT], F32, kind="ExternalInput").ap()
    dout = nc.dram_tensor("out", [T, CD], F32, kind="ExternalOutput").ap()

    ctx = contextlib.ExitStack()
    E = ctx.enter_context

    acc = E(nc.sbuf_tensor("acc", [128, NT, CD], F32))
    lnT = E(nc.sbuf_tensor("lnT", [128, 8, T], BF16))
    gam = E(nc.sbuf_tensor("gam", [128, CD], F32))
    bet = E(nc.sbuf_tensor("bet", [128, CD], F32))
    lamb = E(nc.sbuf_tensor("lamb", [128, 1], F32))
    nlam = E(nc.sbuf_tensor("nlam", [128, 1], F32))
    oml = E(nc.sbuf_tensor("oml", [128, 1], F32))
    idf = E(nc.sbuf_tensor("idf", [128, 128], F32))
    xt = E(nc.sbuf_tensor("xt", [128, 2, CD], F32))
    lnt = E(nc.sbuf_tensor("lnt", [128, 2, CD], F32))
    sums = E(nc.sbuf_tensor("sums", [128, 16], F32))
    invs = E(nc.sbuf_tensor("invs", [128, 16], F32))
    nli = E(nc.sbuf_tensor("nli", [128, 8], F32))
    tmp64 = E(nc.sbuf_tensor("tmp64", [128, 8, HS], F32))
    tmp64b = E(nc.sbuf_tensor("tmp64b", [128, 8, HS], F32))
    lnsc = E(nc.sbuf_tensor("lnsc", [128, 8], F32))
    stat = E(nc.sbuf_tensor("stat", [128, 6, 16], F32))
    gate = E(nc.sbuf_tensor("gate", [128, NT, NRT], F32))
    gl = E(nc.sbuf_tensor("gl", [128, 16], F32))
    gcur = E(nc.sbuf_tensor("gcur", [128, 2, 16], F32))
    gmsk = E(nc.sbuf_tensor("gmsk", [128, 16], F32))
    gsc = E(nc.sbuf_tensor("gsc", [128, 8], F32))
    rwS = E(nc.sbuf_tensor("rwS", [128, 8, NRT], F32))
    rwB = E(nc.sbuf_tensor("rwB", [128, 8, NRT], BF16))

    AR_COLS = 18500
    arena = E(nc.sbuf_tensor("arena", [128, AR_COLS], F32))
    _off = [0]

    def take(cols):
        o = _off[0]
        _off[0] += cols
        assert _off[0] <= AR_COLS, f"arena overflow {_off[0]}"
        return arena[:, o:o + cols]

    def take_bf16(bf16_elems):
        assert bf16_elems % 2 == 0
        return take(bf16_elems // 2).bitcast(BF16)

    qk = [take_bf16(T) for _ in range(4)]
    vext = take_bf16(NT * 130).rearrange("p (n c) -> p n c", n=NT)
    wat = [[take_bf16(8 * 128).rearrange("p (c m) -> p c m", c=8) for _ in range(5)]
           for _ in range(2)]
    wstg = [take(8 * HS).rearrange("p (c d) -> p c d", c=8) for _ in range(4)]
    e12 = [take_bf16(2048) for _ in range(3)]
    e1c = [b[:, 0:1024] for b in e12]
    e2c = [b[:, 1024:2048] for b in e12]
    NEB = 3
    oT1 = take(1024)
    oT2 = take(1024)
    attn_cols = _off[0]
    assert attn_cols <= AR_COLS

    _off[0] = 0
    wup = [take_bf16(8 * ED).rearrange("p (c e) -> p c e", c=8) for _ in range(2)]
    wdn = [take_bf16(4 * CD).rearrange("p (c h) -> p c h", c=4) for _ in range(2)]
    fstg = [take(2 * ED).rearrange("p (c e) -> p c e", c=2) for _ in range(2)]
    actb = [take_bf16(4 * T).rearrange("p (c t) -> p c t", c=4) for _ in range(2)]
    ffn_cols = _off[0]
    assert ffn_cols <= AR_COLS, f"ffn arena {ffn_cols}"
    junk = actb[1].bitcast(F32)[:, 1, :]
    ctmp = actb[0].bitcast(F32)[:, 0, :]

    pAB = E(nc.psum_tensor("pAB", [128, 2048], F32))
    pA = pAB[:, 0:1024]
    pB = pAB[:, 1024:2048]
    pC = E(nc.psum_tensor("pC", [128, 1024], F32))
    pD = E(nc.psum_tensor("pD", [128, 1024], F32))
    pA3 = pA.rearrange("p (n c) -> p n c", n=8)
    pB3 = pB.rearrange("p (n c) -> p n c", n=8)
    pC3 = pC.rearrange("p (n c) -> p n c", n=8)
    pD3 = pD.rearrange("p (n c) -> p n c", n=8)

    prog = {k: [] for k in ("sync", "vector", "scalar", "tensor", "gpsimd")}

    def em(eng, fn):
        prog[eng].append(fn)

    sems = {}

    def mk(name):
        s = _Sem()
        s.h = E(nc.semaphore(name))
        s.n = 0
        sems[name] = s
        return s

    s_const = mk("s_const")
    s_xld = [mk("s_xldA"), mk("s_xldB")]
    s_xfree = mk("s_xfree")
    s_pre = mk("s_pre")
    s_v = mk("s_v")
    s_s = mk("s_s")
    s_lnr = mk("s_lnr")
    s_lnfree = mk("s_lnfree")
    s_lntr = mk("s_lntr")
    s_lnTd = mk("s_lnTd")
    s_wld = [mk("s_wldA"), mk("s_wldB"), mk("s_wldC"), mk("s_wldD")]
    s_wstgf = mk("s_wstgf")
    s_wcast = mk("s_wcast")
    s_wfree = mk("s_wfree")
    s_proj = mk("s_proj")
    s_projd = mk("s_projd")
    s_st = mk("s_st")
    s_exp = mk("s_exp")
    s_av = mk("s_av")
    s_oev = mk("s_oev")
    s_otr = mk("s_otr")
    s_epi = mk("s_epi")
    s_ep1 = mk("s_ep1")
    s_ep2 = mk("s_ep2")
    s_rl = mk("s_rl")
    s_gate = mk("s_gate")
    s_fld = [mk("s_fldA"), mk("s_fldB")]
    s_fstgf = mk("s_fstgf")
    s_fcast = mk("s_fcast")
    s_ffree = mk("s_ffree")
    s_up = mk("s_up")
    s_act = mk("s_act")
    s_dn = mk("s_dn")
    s_gacc = mk("s_gacc")
    s_out = mk("s_out")

    def W(eng, s, val=None):
        v = s.n if val is None else val
        v = max(v, 0)
        em(eng, lambda e, _h=s.h, _v=v: e.wait_ge(_h, _v))

    def I(eng, fn, s=None, k=1):
        if s is None:
            em(eng, fn)
        else:
            s.n += k
            em(eng, lambda e, _f=fn, _h=s.h, _k=k: _f(e).then_inc(_h, _k))

    def DMA(out_ap, in_ap, s):
        I("sync", lambda e, _o=out_ap, _i=in_ap: e.dma_start(out=_o, in_=_i), s, 16)

    DMA(lamb[:], dlamb, s_const)
    DMA(idf[:], dident, s_const)
    DMA(gam[:], dgamma, s_const)
    DMA(bet[:], dbeta, s_const)
    DMA(rwS[:], drw.rearrange("(c p) r -> p c r", p=128), s_const)
    W("scalar", s_const)
    I("scalar", lambda e: e.activation(oml[:], lamb[:], AF.Copy, scale=-1.0, bias=1.0), s_s)
    I("scalar", lambda e: e.activation(nlam[:], lamb[:], AF.Copy, scale=-1.0), s_s)
    W("vector", s_s)
    W("vector", s_const)
    I("vector", lambda e: e.tensor_scalar(gam[:], gam[:], oml[:], None, ALU.mult), s_v)
    I("vector", lambda e: e.tensor_scalar(bet[:], bet[:], oml[:], None, ALU.mult), s_v)
    I("vector", lambda e: e.tensor_copy(rwB[:], rwS[:]), s_pre)
    I("vector", lambda e: e.memset(vext[:], 1.0), s_pre)

    s_sums = stat[:, 0, :]
    s_nmean = stat[:, 1, :]
    s_sqs = stat[:, 2, :]
    s_var = stat[:, 3, :]
    s_rstd = stat[:, 4, :]
    s_nb = stat[:, 5, :]

    def emit_ln_stats_tile(src_ap, tt):
        I("vector", lambda e, _t=tt, _s=src_ap: e.reduce_sum(s_sums[:, _t:_t + 1], _s, axis=AX),
          s_v if tt == NT - 1 else None)
        I("scalar", lambda e, _t=tt, _s=src_ap: e.activation(junk[:], _s, AF.Square,
                                                             accum_out=s_sqs[:, _t:_t + 1]),
          s_s)

    def emit_ln_stats_finish():
        W("vector", s_v)
        W("vector", s_s)
        I("vector", lambda e: e.tensor_scalar(s_nmean[:], s_sums[:], -1.0 / CD, None, ALU.mult), s_v)
        W("vector", s_v)
        I("vector", lambda e: e.tensor_tensor(s_var[:], s_nmean[:], s_nmean[:], ALU.mult), s_v)
        I("vector", lambda e: e.tensor_scalar(s_sqs[:], s_sqs[:], 1.0 / CD, EPS, ALU.mult, ALU.add), s_v)
        W("vector", s_v)
        I("vector", lambda e: e.tensor_sub(s_var[:], s_sqs[:], s_var[:]), s_v)
        W("scalar", s_v)
        I("scalar", lambda e: e.activation(s_var[:], s_var[:], AF.Ln), s_s)
        W("scalar", s_s)
        I("scalar", lambda e: e.activation(s_rstd[:], s_var[:], AF.Exp, scale=-0.5), s_s)
        W("vector", s_s)
        I("vector", lambda e: e.tensor_tensor(s_nb[:], s_nmean[:], s_rstd[:], ALU.mult), s_v)
        W("scalar", s_v)

    def emit_ln_scale_tile(src_ap, tt, lnbuf, transposed=True):
        if transposed and tt >= 2:
            W("scalar", s_lntr, s_lntr.n - 1)
        I("scalar", lambda e, _t=tt, _s=src_ap, _b=lnbuf: e.activation(
            lnt[:, _b, :], _s, AF.Identity, scale=s_rstd[:, _t:_t + 1],
            bias=s_nb[:, _t:_t + 1]), s_lnr)

    def emit_ln_transpose(tt, lnbuf):
        pp = pC3 if tt % 2 == 0 else pD3
        W("tensor", s_lnr)
        if tt >= 2:
            W("tensor", s_lnTd, s_lnTd.n - 16)
        for c in range(8):
            last = (c == 7)
            I("tensor",
              lambda e, _c=c, _pp=pp, _b=lnbuf: e.transpose(
                  _pp[:, _c, :], lnt[:, _b, _c * 128:(_c + 1) * 128], idf[:]),
              s_lntr if last else None)
        W("vector", s_lntr)
        I("vector",
          lambda e, _pp=pp, _t=tt: e.tensor_copy(
              lnT[:, :, _t * 128:(_t + 1) * 128], _pp[:, :, :]),
          s_lnTd, 16)

    W("vector", s_pre)
    for tt in range(NT):
        buf = tt % 2
        if tt >= 2:
            W("sync", s_xfree, s_xfree.n - 1)
        DMA(xt[:, buf, :], dx.rearrange("(n p) c -> p n c", p=128)[:, tt, :], s_xld[buf])
        W("vector", s_xld[buf])
        W("scalar", s_xld[buf])
        emit_ln_stats_tile(xt[:, buf, :], tt)
        W("vector", s_s)
        I("vector", lambda e: e.memset(lnsc[:, 7:8], 0.0), s_xfree)
    emit_ln_stats_finish()
    for tt in range(NT):
        buf = tt % 2
        W("sync", s_xfree, s_xfree.n - 1)
        DMA(xt[:, buf, :], dx.rearrange("(n p) c -> p n c", p=128)[:, tt, :], s_xld[buf])
        W("scalar", s_xld[buf])
        emit_ln_scale_tile(xt[:, buf, :], tt, buf)
        W("vector", s_lnr)
        I("vector", lambda e: e.memset(lnsc[:, 7:8], 0.0), s_xfree)
        emit_ln_transpose(tt, buf)

    WSRC = [dwq1, dwq2, dwk1, dwk2, dwv]

    def emit_attn_wload(p):
        wb = wat[p % 2]
        h0, h1 = 2 * p, 2 * p + 1
        units = [(0, 0, dwq1, h0), (0, 1, dwq2, h0),
                 (1, 0, dwq1, h1), (1, 1, dwq2, h1),
                 (2, 0, dwk1, h0), (2, 1, dwk2, h0),
                 (3, 0, dwk1, h1), (3, 1, dwk2, h1),
                 (4, 0, dwv, h0), (4, 1, dwv, h1)]
        if p >= 1:
            W("sync", s_wfree, p - 1)
        for u, (di, half, src, hh) in enumerate(units):
            sb = (10 * p + u) % 4
            if 10 * p + u >= 4:
                W("sync", s_wstgf, s_wstgf.n - 3)
            DMA(wstg[sb][:], src[hh].rearrange("(c p) d -> p c d", p=128), s_wld[sb])
            W("vector", s_wld[sb])
            if p >= 1:
                W("vector", s_wfree, p - 1)
            I("vector",
              lambda e, _d=wb[di], _h=half, _s=wstg[sb]: e.tensor_copy(
                  _d[:, :, _h * HS:(_h + 1) * HS], _s[:]),
              s_wstgf)
        I("vector", lambda e: e.memset(lnsc[:, 7:8], 0.0), s_wcast)

    def emit_proj(p):
        wb = wat[p % 2]
        W("tensor", s_wcast, p + 1)
        W("tensor", s_exp)
        lnA_base = s_lnTd.n - 16 * NT
        W("vector", s_av)
        nchain = 0
        for ts4 in range(4):
            W("tensor", s_lnTd, lnA_base + 64 * (ts4 + 1))
            for di in range(4):
                half = nchain % 2
                ppo = pA[:, half * 512:(half + 1) * 512]
                if nchain >= 2:
                    W("tensor", s_projd, s_projd.n - 1)
                for hc in range(8):
                    I("tensor",
                      lambda e, _w=wb[di], _hc=hc, _ts=ts4, _o=ppo: e.matmul(
                          _o, _w[:, _hc, :], lnT[:, _hc, _ts * 512:(_ts + 1) * 512],
                          start=(_hc == 0), stop=(_hc == 7)),
                      s_proj if hc == 7 else None)
                W("vector", s_proj)
                I("vector",
                  lambda e, _d=qk[di], _ts=ts4, _o=ppo: e.tensor_copy(
                      _d[:, _ts * 512:(_ts + 1) * 512], _o),
                  s_projd)
                nchain += 1
        I("vector", lambda e: e.memset(vext[:, :, 64:65], 1.0), s_v)
        I("vector", lambda e: e.memset(vext[:, :, 129:130], 1.0), s_v)
        for tt in range(NT):
            half = tt % 2
            ppo = pB[:, half * 512:half * 512 + 128]
            W("tensor", s_lnTd, lnA_base + 16 * (tt + 1))
            if tt >= 2:
                W("tensor", s_projd, s_projd.n - 1)
            for hc in range(8):
                I("tensor",
                  lambda e, _hc=hc, _tt=tt, _o=ppo: e.matmul(
                      _o, lnT[:, _hc, _tt * 128:(_tt + 1) * 128], wb[4][:, _hc, :],
                      start=(_hc == 0), stop=(_hc == 7)),
                  s_proj if hc == 7 else None)
            W("vector", s_proj)
            I("vector",
              lambda e, _tt=tt, _o=ppo: e.tensor_copy(vext[:, _tt, 0:64], _o[:, 0:64]),
              None)
            I("vector",
              lambda e, _tt=tt, _o=ppo: e.tensor_copy(vext[:, _tt, 65:129], _o[:, 64:128]),
              s_projd)
        I("tensor", lambda e: e.nop(), s_wfree)

    def emit_attn_inner(p, hl, Hh, grp):
        qT = qk[hl]
        kT = qk[2 + hl]
        h = 2 * p + hl
        tbase = Hh * 1024
        vlo, vhi = (0, 65) if hl == 0 else (65, 130)
        for sc in range(17):
            scp = sc - 1
            buf = scp % NEB
            if sc < 16:
                if sc == 0:
                    W("tensor", s_projd)
                W("tensor", s_exp, s_exp.n - 1)
                for half in range(2):
                    I("tensor",
                      lambda e, _sc=sc, _hf=half: e.matmul(
                          pA[:, _hf * 512:(_hf + 1) * 512],
                          kT[0:64, _sc * 128:(_sc + 1) * 128],
                          qT[0:64, tbase + _hf * 512: tbase + (_hf + 1) * 512],
                          start=True, stop=True),
                      s_st if half == 1 else None)
                W("tensor", s_exp)
                for half in range(2):
                    I("tensor",
                      lambda e, _sc=sc, _hf=half: e.matmul(
                          pB[:, _hf * 512:(_hf + 1) * 512],
                          kT[64:128, _sc * 128:(_sc + 1) * 128],
                          qT[64:128, tbase + _hf * 512: tbase + (_hf + 1) * 512],
                          start=True, stop=True),
                      s_st if half == 1 else None)
            if sc >= 1:
                if sc == 16:
                    W("tensor", s_exp)
                if scp == 0:
                    if grp == 0:
                        W("tensor", s_lnTd)
                    W("tensor", s_oev)
                    W("tensor", s_ep1)
                    W("tensor", s_ep2)
                for half in range(2):
                    I("tensor",
                      lambda e, _sc=scp, _hf=half, _b=buf: e.matmul(
                          pC[0:65, _hf * 512:(_hf + 1) * 512],
                          vext[:, _sc, vlo:vhi],
                          e1c[_b][:, _hf * 512:(_hf + 1) * 512],
                          start=(_sc == 0), stop=(_sc == 15)),
                      None)
                for half in range(2):
                    I("tensor",
                      lambda e, _sc=scp, _hf=half, _b=buf: e.matmul(
                          pD[0:65, _hf * 512:(_hf + 1) * 512],
                          vext[:, _sc, vlo:vhi],
                          e2c[_b][:, _hf * 512:(_hf + 1) * 512],
                          start=(_sc == 0), stop=(_sc == 15)),
                      s_av if half == 1 else None)
            if sc < 16:
                W("scalar", s_st, s_st.n - 1)
                if sc == 0:
                    W("scalar", s_av, s_av.n)
                elif sc == 1:
                    W("scalar", s_av, s_av.n - 1)
                else:
                    W("scalar", s_av, s_av.n - 2)
                I("scalar",
                  lambda e, _b=sc % NEB: e.activation(e1c[_b][:], pA[:, :], AF.Exp,
                                                      scale=0.125),
                  s_exp)
                W("scalar", s_st)
                I("scalar",
                  lambda e, _b=sc % NEB: e.activation(e2c[_b][:], pB[:, :], AF.Exp,
                                                      scale=0.125),
                  s_exp)
        W("vector", s_av)
        I("vector", lambda e: e.tensor_copy(oT1[0:65, :], pC[0:65, :]), None)
        I("vector", lambda e: e.tensor_copy(oT2[0:65, :], pD[0:65, :]), s_oev)
        W("tensor", s_oev)
        for j in range(8):
            I("tensor",
              lambda e, _j=j: e.transpose(pC3[:, _j, 0:65],
                                          oT1[0:65, _j * 128:(_j + 1) * 128],
                                          idf[0:65, 0:65]),
              None)
        for j in range(8):
            I("tensor",
              lambda e, _j=j: e.transpose(pD3[:, _j, 0:65],
                                          oT2[0:65, _j * 128:(_j + 1) * 128],
                                          idf[0:65, 0:65]),
              s_otr if j == 7 else None)
        W("vector", s_otr)
        I("vector", lambda e: e.tensor_copy(sums[:, 0:8], pC3[:, :, 64:65].rearrange("p n c -> p (n c)")), None)
        I("vector", lambda e: e.tensor_copy(sums[:, 8:16], pD3[:, :, 64:65].rearrange("p n c -> p (n c)")), s_v)
        W("vector", s_v)
        I("vector", lambda e: e.reciprocal(invs[:], sums[:]), s_v)
        W("vector", s_v)
        I("vector", lambda e: e.tensor_scalar(nli[:], invs[:, 8:16], nlam[:], None, ALU.mult), s_v)
        W("vector", s_v)
        I("vector", lambda e: e.tensor_tensor(
            tmp64[:], pC3[:, :, 0:64],
            invs[:, 0:8].rearrange("p n -> p n ()").broadcast_to([128, 8, HS]),
            ALU.mult), s_ep1)
        I("vector", lambda e: e.tensor_tensor(
            tmp64b[:], pD3[:, :, 0:64],
            nli[:].rearrange("p n -> p n ()").broadcast_to([128, 8, HS]),
            ALU.mult), s_ep2)
        W("vector", s_ep1)
        W("vector", s_ep2)
        I("vector", lambda e: e.tensor_tensor(
            acc[:, Hh * 8:(Hh + 1) * 8, h * HS:(h + 1) * HS],
            tmp64[:], tmp64b[:], ALU.add), s_epi)

    for p in range(8):
        emit_attn_wload(p)
        emit_proj(p)
        for hl in range(2):
            for Hh in range(2):
                emit_attn_inner(p, hl, Hh, p * 4 + hl * 2 + Hh)

    W("sync", s_epi)
    W("vector", s_epi)
    W("scalar", s_epi)
    for tt in range(NT):
        emit_ln_stats_tile(acc[:, tt, :], tt)
    emit_ln_stats_finish()
    for tt in range(NT):
        buf = tt % 2
        W("sync", s_xfree, s_xfree.n if tt < 2 else s_xfree.n - 1)
        DMA(xt[:, buf, :], dx.rearrange("(n p) c -> p n c", p=128)[:, tt, :], s_xld[buf])
        a_t = acc[:, tt, :]
        if tt >= 2:
            W("scalar", s_v, s_v.n - 3)
        emit_ln_scale_tile(a_t, tt, buf, transposed=False)
        W("vector", s_lnr)
        W("vector", s_xld[buf])
        I("vector", lambda e, _b=buf: e.tensor_tensor(lnt[:, _b, :], lnt[:, _b, :], gam[:], ALU.mult), s_v)
        I("vector", lambda e, _b=buf: e.tensor_tensor(ctmp[:], xt[:, _b, :], bet[:], ALU.add), s_v)
        W("vector", s_v)
        I("vector", lambda e, _a=a_t, _b=buf: e.tensor_tensor(_a, lnt[:, _b, :], ctmp[:], ALU.add), s_v)
        I("vector", lambda e: e.memset(lnsc[:, 7:8], 0.0), s_xfree)
        W("scalar", s_v)
        emit_ln_stats_tile(acc[:, tt, :], tt)
    emit_ln_stats_finish()
    for tt in range(NT):
        buf = tt % 2
        emit_ln_scale_tile(acc[:, tt, :], tt, buf)
        emit_ln_transpose(tt, buf)

    W("tensor", s_lnTd)
    W("tensor", s_lnr)
    for tt in range(NT):
        slot = pB3[:, tt // 2, (tt % 2) * 32:(tt % 2) * 32 + NRT]
        for hc in range(8):
            I("tensor",
              lambda e, _hc=hc, _tt=tt, _sl=slot: e.matmul(
                  _sl, lnT[:, _hc, _tt * 128:(_tt + 1) * 128], rwB[:, _hc, :],
                  start=(_hc == 0), stop=(_hc == 7)),
              s_rl if hc == 7 else None)
    glL = gl3 = None
    pslots = pB.rearrange("p (q b x) -> p q b x", q=8, b=4)[:, :, 0:2, 0:NRT]
    gall = gate
    gm1 = E(nc.sbuf_tensor("gm1", [128, NT, 1], F32))
    gcur2 = E(nc.sbuf_tensor("gcur2", [128, NT, NRT], F32))
    gmskb = E(nc.sbuf_tensor("gmskb", [128, NT, NRT], F32))
    gsums = E(nc.sbuf_tensor("gsums", [128, NT, 1], F32))
    ginv = E(nc.sbuf_tensor("ginv", [128, NT, 1], F32))
    W("vector", s_rl)
    ga3 = gall[:]
    I("vector", lambda e: e.tensor_copy(ga3, pslots), s_v)
    W("vector", s_v)
    I("vector", lambda e: e.reduce_max(gm1[:], ga3, axis=AX), s_v)
    W("vector", s_v)
    I("vector", lambda e: e.tensor_sub(gcur2[:], ga3, gm1[:].broadcast_to([128, NT, NRT])), s_v)
    W("scalar", s_v)
    I("scalar", lambda e: e.activation(ga3, gcur2[:], AF.Exp), s_s)
    W("vector", s_s)
    I("vector", lambda e: e.reduce_sum(gsums[:], ga3, axis=AX), s_v)
    I("vector", lambda e: e.tensor_copy(gcur2[:], ga3), s_v)
    W("vector", s_v)
    for it in range(3):
        I("vector", lambda e: e.reduce_max(gm1[:], gcur2[:], axis=AX), s_v)
        W("vector", s_v)
        I("vector", lambda e: e.tensor_tensor(gmskb[:], gcur2[:],
                                              gm1[:].broadcast_to([128, NT, NRT]), ALU.is_ge), s_v)
        W("vector", s_v)
        I("vector", lambda e: e.scalar_tensor_tensor(gcur2[:], gmskb[:], -2.0, gcur2[:],
                                                     ALU.mult, ALU.add), s_v)
        W("vector", s_v)
    I("vector", lambda e: e.reduce_max(gm1[:], gcur2[:], axis=AX), s_v)
    I("vector", lambda e: e.reciprocal(ginv[:], gsums[:]), s_v)
    W("vector", s_v)
    I("vector", lambda e: e.tensor_tensor(gmskb[:], ga3,
                                          gm1[:].broadcast_to([128, NT, NRT]), ALU.is_ge), s_v)
    W("vector", s_v)
    I("vector", lambda e: e.tensor_mul(gmskb[:], gmskb[:], ga3), s_v)
    W("vector", s_v)
    I("vector", lambda e: e.tensor_tensor(ga3, gmskb[:],
                                          ginv[:].broadcast_to([128, NT, NRT]), ALU.mult), s_gate)

    def emit_ffn_wload(r, src_up, src_dn, idx):
        wb_u, wb_d = wup[r % 2], wdn[r % 2]
        if r >= 2:
            W("sync", s_ffree, r - 1)
        for q in range(4):
            sb = q % 2
            if r > 0 or q >= 2:
                W("sync", s_fstgf, s_fstgf.n - 1)
            DMA(fstg[sb][:], src_up[idx][q * 256:(q + 1) * 256].rearrange(
                "(c p) e -> p c e", p=128), s_fld[sb])
            W("vector", s_fld[sb])
            if r >= 2:
                W("vector", s_ffree, r - 1)
            I("vector",
              lambda e, _q=q, _sb=sb, _w=wb_u: e.tensor_copy(
                  _w[:, _q * 2:(_q + 1) * 2, :], fstg[_sb][:]),
              s_fstgf)
        for q in range(4):
            sb = q % 2
            W("sync", s_fstgf, s_fstgf.n - 1)
            DMA(fstg[sb][:].rearrange("p c e -> p (c e)"),
                src_dn[idx][q * 128:(q + 1) * 128].rearrange("(c p) h -> p (c h)", p=128),
                s_fld[sb])
            W("vector", s_fld[sb])
            I("vector",
              lambda e, _q=q, _sb=sb, _w=wb_d: e.tensor_copy(
                  _w[:, _q, :], fstg[_sb][:].rearrange("p c e -> p (c e)")),
              s_fstgf)
        I("vector", lambda e: e.memset(lnsc[:, 7:8], 0.0), s_fcast)

    def emit_expert(r, gated, gcol_idx):
        wb_u, wb_d = wup[r % 2], wdn[r % 2]
        ab = actb[r % 2]
        W("tensor", s_fcast, r + 1)
        if r == 0:
            W("tensor", s_lnTd)
            W("tensor", s_epi)
        nup = 0
        for tc in range(4):
            for es in range(4):
                half = nup % 2
                ppo = pA[:, half * 512:(half + 1) * 512]
                if r > 0 or nup >= 2:
                    W("tensor", s_act, s_act.n - 1)
                if r >= 2 and nup == 0:
                    W("tensor", s_dn)
                for hc in range(8):
                    I("tensor",
                      lambda e, _hc=hc, _es=es, _tc=tc, _o=ppo: e.matmul(
                          _o, wb_u[:, _hc, _es * 128:(_es + 1) * 128],
                          lnT[:, _hc, _tc * 512:(_tc + 1) * 512],
                          start=(_hc == 0), stop=(_hc == 7)),
                      s_up if hc == 7 else None)
                W("scalar", s_up)
                if r >= 2 and nup == 0:
                    W("scalar", s_dn)
                I("scalar",
                  lambda e, _es=es, _tc=tc, _o=ppo, _a=ab: e.activation(
                      _a[:, _es, _tc * 512:(_tc + 1) * 512], _o, AF.Gelu),
                  s_act)
                nup += 1
        act_base = s_act.n - 16
        for tt in range(NT):
            pp = pC if tt % 2 == 0 else pD
            W("tensor", s_act, act_base + 4 * (tt // 4 + 1))
            if tt < 2:
                if r > 0:
                    W("tensor", s_gacc)
                else:
                    W("tensor", s_oev)
            else:
                W("tensor", s_gacc, s_gacc.n - 1)
            for half in range(2):
                for ec in range(4):
                    I("tensor",
                      lambda e, _ec=ec, _tt=tt, _hf=half, _pp=pp: e.matmul(
                          _pp[:, _hf * 512:(_hf + 1) * 512],
                          ab[:, _ec, _tt * 128:(_tt + 1) * 128],
                          wb_d[:, _ec, _hf * 512:(_hf + 1) * 512],
                          start=(_ec == 0), stop=(_ec == 3)),
                      s_dn if (half == 1 and ec == 3) else None)
            W("vector", s_dn)
            if gated:
                I("vector",
                  lambda e, _tt=tt, _pp=pp, _g=gcol_idx: e.scalar_tensor_tensor(
                      acc[:, _tt, :], _pp[:], gate[:, _tt, _g:_g + 1], acc[:, _tt, :],
                      ALU.mult, ALU.add),
                  s_gacc)
            else:
                I("vector",
                  lambda e, _tt=tt, _pp=pp: e.tensor_tensor(
                      acc[:, _tt, :], acc[:, _tt, :], _pp[:], ALU.add),
                  s_gacc)
        I("tensor", lambda e: e.nop(), s_ffree)

    W("vector", s_av)
    W("vector", s_otr)
    W("sync", s_epi)
    W("sync", s_otr)
    order = [("s", 0), ("s", 1)] + [("r", i) for i in range(NRT)]

    def wl(r):
        kind, idx = order[r]
        if kind == "s":
            emit_ffn_wload(r, dsup, dsdn, idx)
        else:
            emit_ffn_wload(r, drup, drdn, idx)

    wl(0)
    wl(1)
    for r, (kind, idx) in enumerate(order):
        if r == 2:
            W("vector", s_gate)
        emit_expert(r, gated=(kind == "r"), gcol_idx=idx)
        if r + 2 < len(order):
            wl(r + 2)

    gacc_base = s_gacc.n - NT
    for tt in range(NT):
        W("sync", s_gacc, gacc_base + tt + 1)
        DMA(dout.rearrange("(n p) c -> p n c", p=128)[:, tt, :], acc[:, tt, :], s_out)
    W("sync", s_out)

    with nc.Block() as block:
        @block.sync
        def _(e):
            for f in prog["sync"]:
                f(e)

        @block.vector
        def _(e):
            for f in prog["vector"]:
                f(e)

        @block.scalar
        def _(e):
            for f in prog["scalar"]:
                f(e)

        @block.tensor
        def _(e):
            for f in prog["tensor"]:
                f(e)

        @block.gpsimd
        def _(e):
            for f in prog["gpsimd"]:
                f(e)

    ctx.close()
    return nc


_CACHE = {}


def kernel(L0_attn_x, L0_attn_wq1, L0_attn_wk1, L0_attn_wq2, L0_attn_wk2, L0_attn_wv,
           L0_attn_ln_gamma, L0_attn_ln_beta, L0_attn_lamb,
           L0_ffn_shared_up, L0_ffn_shared_down, L0_ffn_routed_up, L0_ffn_routed_down,
           L0_ffn_router_w):
    x = np.ascontiguousarray(np.asarray(L0_attn_x, dtype=np.float32))
    B = x.shape[0]
    common = {
        "wq1": np.ascontiguousarray(L0_attn_wq1, np.float32),
        "wk1": np.ascontiguousarray(L0_attn_wk1, np.float32),
        "wq2": np.ascontiguousarray(L0_attn_wq2, np.float32),
        "wk2": np.ascontiguousarray(L0_attn_wk2, np.float32),
        "wv": np.ascontiguousarray(L0_attn_wv, np.float32),
        "gammab": np.ascontiguousarray(
            np.broadcast_to(np.asarray(L0_attn_ln_gamma, np.float32)[None, :], (128, CD))),
        "betab": np.ascontiguousarray(
            np.broadcast_to(np.asarray(L0_attn_ln_beta, np.float32)[None, :], (128, CD))),
        "lambcol": np.ascontiguousarray(
            np.broadcast_to(np.asarray(L0_attn_lamb, np.float32).reshape(1, 1), (128, 1))),
        "ident": np.eye(128, dtype=np.float32),
        "supT": np.ascontiguousarray(np.asarray(L0_ffn_shared_up, np.float32).transpose(0, 2, 1)),
        "sdnT": np.ascontiguousarray(np.asarray(L0_ffn_shared_down, np.float32).transpose(0, 2, 1)),
        "rupT": np.ascontiguousarray(np.asarray(L0_ffn_routed_up, np.float32).transpose(0, 2, 1)),
        "rdnT": np.ascontiguousarray(np.asarray(L0_ffn_routed_down, np.float32).transpose(0, 2, 1)),
        "rwT": np.ascontiguousarray(np.asarray(L0_ffn_router_w, np.float32).T),
    }
    if "nc" not in _CACHE:
        _CACHE["nc"] = build_program()
    nc = _CACHE["nc"]
    in_maps = [dict(common, x=np.ascontiguousarray(x[b])) for b in range(B)]
    res = run_bass_kernel_spmd(nc, in_maps, core_ids=list(range(B)))
    return np.stack([res.results[b]["out"] for b in range(B)], axis=0)


if __name__ == "__main__":
    np.random.seed(0)
    import reference as ref
    inp = ref.setup_inputs()
    inp = {k: np.asarray(v) for k, v in inp.items()}
    got = kernel(**inp)
    exp = np.asarray(ref.reference(**{k: v for k, v in inp.items()}))
    err = np.sqrt(np.mean((got - exp) ** 2) / np.mean(exp ** 2))
    print("rel l2:", err, "max abs diff:", np.abs(got - exp).max())
